# version 01 (final)
# baseline (speedup 1.0000x reference)
import numpy as np
import ml_dtypes
import concourse.bass as bass
import concourse.mybir as mybir
from concourse.bass_utils import run_bass_kernel_spmd
from concourse.bass_core_rust import IndirectOffsetOnAxis

F32 = mybir.dt.float32
BF16 = mybir.dt.bfloat16
I32 = mybir.dt.int32
AF = mybir.ActivationFunctionType
ALU = mybir.AluOpType
AX = mybir.AxisListType

NCORES = 8
D = 2048
NB = 2
S = 2048
T = NB * S
KC = D // 128
EPS_LN = 1e-5

BF = ml_dtypes.bfloat16

ENGINES = ("sync", "tensor", "vector", "scalar", "gpsimd")


class B:

    def __init__(self, name="prog"):
        self.nc = bass.Bass("TRN2", target_bir_lowering=False)
        self.name = name
        self.ops = {e: [] for e in ENGINES}
        self._sems = {}
        self._cnt = {}
        self._ctxs = []
        self._consts = {}
        self._const_sb = None

    def constf(self, val):
        val = float(val)
        if val in self._consts:
            return
        if self._const_sb is None:
            self._const_sb = self.sb("constf", [128, 16], F32)
        col = len(self._consts)
        assert col < 16
        ap = self._const_sb[:, col:col + 1]
        self._consts[val] = ap
        self.nc.const_aps.register(F32, val, ap)

    def sem(self, name):
        if name not in self._sems:
            g = self.nc.semaphore(name)
            self._ctxs.append(g)
            self._sems[name] = g.__enter__()
            self._cnt[name] = 0
        return self._sems[name]

    def inc(self, name, n=1):
        h = self.sem(name)
        self._cnt[name] += n
        return h, self._cnt[name]

    def val(self, name):
        self.sem(name)
        return self._cnt[name]

    def on(self, eng, fn):
        self.ops[eng].append(fn)

    def sb(self, name, shape, dtype):
        g = self.nc.sbuf_tensor(name, shape, dtype)
        self._ctxs.append(g)
        return g.__enter__()

    def ps(self, name, shape, dtype=F32):
        g = self.nc.psum_tensor(name, shape, dtype)
        self._ctxs.append(g)
        return g.__enter__()

    def finish(self):
        pre = {e: [] for e in ENGINES}
        if self._consts:
            h = self.sem("cst")

            def setup(q, consts=dict(self._consts), h=h):
                last = None
                for val, ap in consts.items():
                    last = q.memset(ap, val)
                last.then_inc(h, 1)

            pre["gpsimd"].append(setup)
            for e in ("scalar", "vector"):
                pre[e].append(lambda q, h=h: q.wait_ge(h, 1))
        with self.nc.Block() as blk:
            for eng in ENGINES:
                ops = pre[eng] + self.ops[eng]

                def run(q, _ops=ops):
                    for fn in _ops:
                        fn(q)

                getattr(blk, eng)(run)
        for c in reversed(self._ctxs):
            c.__exit__(None, None, None)
        return self.nc



def wait(b, eng, tag, v=None):
    if v is None:
        v = b.val(tag)
    if v <= 0:
        return
    h = b.sem(tag)
    b.on(eng, lambda q, h=h, v=v: q.wait_ge(h, v))


def dma(b, eng, tag, out, in_, **kw):
    h, v = b.inc(tag, 16)
    b.on(eng, lambda q, o=out, i=in_, h=h, kw=kw:
         q.dma_start(out=o, in_=i, **kw).then_inc(h, 16))
    return v


def act(b, tag, out, in_, func, bias=0.0, scale=1.0, accum_out=None, eng="scalar"):
    h, v = b.inc(tag)
    b.on(eng, lambda q, o=out, i=in_, f=func, bi=bias, sc=scale, ao=accum_out, h=h:
         q.activation(out=o, in_=i, func=f, bias=bi, scale=sc, accum_out=ao).then_inc(h, 1))
    return v


def tt(b, eng, tag, out, in0, in1, op):
    h, v = b.inc(tag)
    b.on(eng, lambda q, o=out, a=in0, c=in1, op=op, h=h:
         q.tensor_tensor(out=o, in0=a, in1=c, op=op).then_inc(h, 1))
    return v


def tcopy(b, eng, tag, out, in_):
    h, v = b.inc(tag)
    b.on(eng, lambda q, o=out, i=in_, h=h: q.tensor_copy(o, i).then_inc(h, 1))
    return v


def recip(b, eng, tag, out, in_):
    h, v = b.inc(tag)
    b.on(eng, lambda q, o=out, i=in_, h=h: q.reciprocal(o, i).then_inc(h, 1))
    return v


def tscalar(b, eng, tag, out, in0, s1, s2, op0, op1=ALU.bypass, accum_out=None):
    h, v = b.inc(tag)
    b.on(eng, lambda q, o=out, i=in0, s1=s1, s2=s2, op0=op0, op1=op1, ao=accum_out, h=h:
         q.tensor_scalar(out=o, in0=i, scalar1=s1, scalar2=s2, op0=op0, op1=op1,
                         accum_out=ao).then_inc(h, 1))
    return v


def stt(b, eng, tag, out, in0, scalar, in1, op0, op1):
    h, v = b.inc(tag)
    b.on(eng, lambda q, o=out, a=in0, s=scalar, c=in1, op0=op0, op1=op1, h=h:
         q.scalar_tensor_tensor(out=o, in0=a, scalar=s, in1=c, op0=op0, op1=op1).then_inc(h, 1))
    return v


def mm(b, tag, out, lhsT, rhs, start, stop, inc=False):
    if inc:
        h, v = b.inc(tag)
        b.on("tensor", lambda q, o=out, l=lhsT, r=rhs, st=start, sp=stop, h=h:
             q.matmul(o, l, r, start=st, stop=sp).then_inc(h, 1))
        return v
    b.on("tensor", lambda q, o=out, l=lhsT, r=rhs, st=start, sp=stop:
         q.matmul(o, l, r, start=st, stop=sp))
    return None


def memset(b, eng, tag, ap, val):
    h, v = b.inc(tag)
    b.on(eng, lambda q, a=ap, val=val, h=h: q.memset(a, val).then_inc(h, 1))
    return v


class Rot:

    def __init__(self, n):
        self.last = [None] * n

    def acquire(self, b, eng, i):
        g = self.last[i % len(self.last)]
        if g is not None:
            wait(b, eng, g[0], g[1])

    def release(self, i, tag, v):
        self.last[i % len(self.last)] = (tag, v)



def emit_ln(b, ti, x_hbm, tok0, TG, dst=None, dst_guard=None):
    assert TG == 512
    xb, x2 = ti["xb"], ti["x2"]
    ones = ti["ones"]
    mu, ms, var = ti["mu"], ti["ms"], ti["var"]
    nmu, rstdb = ti["nmu"], ti["rstdb"]
    xln = dst if dst is not None else ti["xln"]
    p_s, p_s2 = ti["p_ln"][0], ti["p_ln"][1]

    wait(b, "gpsimd", "ev", ti.get("xb_free", 0))
    wait(b, "tensor", "st_ev", ti.get("pln_free", 0))
    if "stats_guard" in ti:
        wait(b, "vector", ti["stats_guard"][0], ti["stats_guard"][1])
        wait(b, "scalar", ti["stats_guard"][0], ti["stats_guard"][1])
    for c in range(KC):
        dma(b, "gpsimd", "xbld", xb[:, c, :],
            x_hbm[c * 128:(c + 1) * 128, tok0:tok0 + TG])
    v_ld = b.val("xbld")
    wait(b, "vector", "xbld", v_ld)
    wait(b, "tensor", "xbld", v_ld)
    for c in range(KC):
        if c >= 2:
            wait(b, "vector", "mm", ti.get("mm0", 0) + c - 1)
        tt(b, "vector", "sqe", x2[:, c % 2, :], xb[:, c, :], xb[:, c, :], ALU.mult)
        mm(b, None, p_s[:, :], ones[:, :], xb[:, c, :], start=(c == 0), stop=(c == KC - 1))
        wait(b, "tensor", "sqe", b.val("sqe"))
        mm(b, "mm", p_s2[:, :], ones[:, :], x2[:, c % 2, :], start=(c == 0),
           stop=(c == KC - 1), inc=True)
    ti["mm0"] = b.val("mm")
    wait(b, "scalar", "mm")
    act(b, "st_ev", mu[:, :], p_s[:, :], AF.Copy, scale=1.0 / D)
    act(b, "st_ev", nmu[:, :], p_s[:, :], AF.Copy, scale=-1.0 / D)
    act(b, "st_ev", ms[:, :], p_s2[:, :], AF.Copy, scale=1.0 / D)
    ti["pln_free"] = b.val("st_ev")
    wait(b, "vector", "st_ev")
    tt(b, "vector", "ev", var[:, :], mu[:, :], mu[:, :], ALU.mult)
    wait(b, "vector", "ev")
    tt(b, "vector", "ev", var[:, :], ms[:, :], var[:, :], ALU.subtract)
    wait(b, "scalar", "ev")
    act(b, "st_ev", var[:, :], var[:, :], AF.Sqrt, bias=EPS_LN)
    wait(b, "vector", "st_ev")
    recip(b, "vector", "ev", var[:, :], var[:, :])
    wait(b, "vector", "ev")
    tcopy(b, "vector", "ev", rstdb[:, :], var[:, :])
    wait(b, "vector", "ev")
    if dst_guard is not None:
        wait(b, "vector", dst_guard[0], dst_guard[1])
    for c in range(KC):
        tt(b, "vector", "ev", xln[:, c, :], xb[:, c, :], nmu[:, :], ALU.add)
    wait(b, "vector", "ev")
    for c in range(KC):
        tt(b, "vector", "ev", xln[:, c, :], xln[:, c, :], rstdb[:, :], ALU.mult)
    v = b.val("ev")
    ti["xb_free"] = v
    wait(b, "tensor", "ev", v)
    return v


def make_ln_tiles(b, TG=512, with_xln=True):
    d = {
        "xb": b.sb("ln_xb", [128, KC, TG], BF16),
        "x2": b.sb("ln_x2", [128, 2, TG], BF16),
        "ones": b.sb("ln_ones", [128, 128], BF16),
        "mu": b.sb("ln_mu", [128, TG], F32),
        "ms": b.sb("ln_ms", [128, TG], F32),
        "var": b.sb("ln_var", [128, TG], F32),
        "nmu": b.sb("ln_nmu", [128, TG], BF16),
        "rstdb": b.sb("ln_rstdb", [128, TG], BF16),
        "p_ln": [b.ps(f"p_ln{i}", [128, 512]) for i in range(2)],
    }
    if with_xln:
        d["xln"] = b.sb("ln_xln", [128, KC, TG], BF16)
    return d



def build_mlp(debug=False):
    b = B("mlp")
    b.constf(EPS_LN)
    nc = b.nc
    x_in = nc.dram_tensor("x_in", [D, T], F32, kind="ExternalInput").ap()
    w1d = nc.dram_tensor("w1d", [32, 128, 16, 128], F32, kind="ExternalInput").ap()
    w2d = nc.dram_tensor("w2d", [32, 128, 32, 128], F32, kind="ExternalInput").ap()
    w3d = nc.dram_tensor("w3d", [16, 128, 32, 128], F32, kind="ExternalInput").ap()
    b1m = nc.dram_tensor("b1m", [128, 32], F32, kind="ExternalInput").ap()
    b2m = nc.dram_tensor("b2m", [128, 32], F32, kind="ExternalInput").ap()
    b3m = nc.dram_tensor("b3m", [128, 16], F32, kind="ExternalInput").ap()
    x_out = nc.dram_tensor("x_out", [D, T], F32, kind="ExternalOutput").ap()
    if debug:
        xln_dbg = nc.dram_tensor("xln_dbg", [D, T], BF16, kind="ExternalOutput").ap()
        y1_dbg = nc.dram_tensor("y1_dbg", [4096, T], BF16, kind="ExternalOutput").ap()
        y2_dbg = nc.dram_tensor("y2_dbg", [4096, T], BF16, kind="ExternalOutput").ap()

    TG = 512
    NG = T // TG
    NU = TG // 512

    ti = make_ln_tiles(b, TG)
    y1 = b.sb("y1", [128, 32, TG], BF16)
    y2 = b.sb("y2", [128, 32, TG], BF16)
    wb1 = [b.sb(f"wb1_{i}", [128, 16, 128], BF16) for i in range(2)]
    wb2 = [b.sb(f"wb2_{i}", [128, 32, 128], BF16) for i in range(2)]
    wb3 = [b.sb(f"wb3_{i}", [128, 32, 128], BF16) for i in range(2)]
    b1s = b.sb("b1s", [128, 32], F32)
    b2s = b.sb("b2s", [128, 32], F32)
    b3s = b.sb("b3s", [128, 16], F32)
    xre = [b.sb(f"xre{i}", [128, TG], F32) for i in range(2)]
    xo = [b.sb(f"xo{i}", [128, TG], F32) for i in range(2)]
    p_y = [b.ps(f"p_y{i}", [128, 512]) for i in range(4)]

    v_b = dma(b, "sync", "xld", b1s[:, :], b1m)
    v_b = dma(b, "sync", "xld", b2s[:, :], b2m)
    v_b = dma(b, "sync", "xld", b3s[:, :], b3m)
    memset(b, "gpsimd", "ev", ti["ones"][:, :], 1.0)
    wait(b, "tensor", "ev")
    wait(b, "scalar", "xld", v_b)
    wait(b, "vector", "xld", v_b)

    r_wb1, r_wb2, r_wb3 = Rot(2), Rot(2), Rot(2)
    r_py = Rot(2)
    r_xre, r_xo = Rot(2), Rot(2)

    def dense(src, wtiles, wbuf, r_wb, bias_sb, nin_c, nout_c, dst, bias_func):
        for oc in range(nout_c):
            par = oc % 2
            r_wb.acquire(b, "gpsimd", par)
            vw = dma(b, "gpsimd", "wld", wbuf[par][:, :, :], wtiles[oc])
            wait(b, "tensor", "wld", vw)
            r_py.acquire(b, "tensor", par)
            for kc in range(nin_c):
                for u in range(NU):
                    last = kc == nin_c - 1 and u == NU - 1
                    mm(b, "mmw", p_y[2 * par + u][:, :], wbuf[par][:, kc, :],
                       src[:, kc, u * 512:(u + 1) * 512],
                       start=(kc == 0), stop=(kc == nin_c - 1), inc=last)
            r_wb.release(par, "mmw", b.val("mmw"))
            wait(b, "scalar", "mmw")
            for u in range(NU):
                act(b, "evW", dst[:, oc, u * 512:(u + 1) * 512], p_y[2 * par + u][:, :],
                    bias_func, bias=bias_sb[:, oc:oc + 1])
            r_py.release(par, "evW", b.val("evW"))

    for g in range(NG):
        tok0 = g * TG
        emit_ln(b, ti, x_in, tok0, TG, dst_guard=("mmw", b.val("mmw")))
        if debug:
            wait(b, "scalar", "ev", ti["xb_free"])
            for c in range(KC):
                dma(b, "scalar", "ost", xln_dbg[c * 128:(c + 1) * 128, tok0:tok0 + TG],
                    ti["xln"][:, c, :])
        dense(ti["xln"], [w1d[oc] for oc in range(32)], wb1, r_wb1, b1s, KC, 32, y1, AF.Relu)
        wait(b, "tensor", "evW")
        if debug:
            wait(b, "scalar", "evW")
            for c in range(32):
                dma(b, "scalar", "ost", y1_dbg[c * 128:(c + 1) * 128, tok0:tok0 + TG],
                    y1[:, c, :])
        dense(y1, [w2d[oc] for oc in range(32)], wb2, r_wb2, b2s, 32, 32, y2, AF.Relu)
        wait(b, "tensor", "evW")
        if debug:
            wait(b, "scalar", "evW")
            for c in range(32):
                dma(b, "scalar", "ost", y2_dbg[c * 128:(c + 1) * 128, tok0:tok0 + TG],
                    y2[:, c, :])
        for oc in range(KC):
            par = oc % 2
            r_wb3.acquire(b, "gpsimd", par)
            vw = dma(b, "gpsimd", "wld", wb3[par][:, :, :], w3d[oc])
            r_xre.acquire(b, "sync", par)
            vx = dma(b, "sync", "xld", xre[par][:, :],
                     x_in[oc * 128:(oc + 1) * 128, tok0:tok0 + TG])
            wait(b, "tensor", "wld", vw)
            r_py.acquire(b, "tensor", par)
            for kc in range(32):
                for u in range(NU):
                    last = kc == 31 and u == NU - 1
                    mm(b, "mmw", p_y[2 * par + u][:, :], wb3[par][:, kc, :],
                       y2[:, kc, u * 512:(u + 1) * 512],
                       start=(kc == 0), stop=(kc == 31), inc=last)
            r_wb3.release(par, "mmw", b.val("mmw"))
            wait(b, "vector", "mmw")
            wait(b, "vector", "xld", vx)
            r_xo.acquire(b, "vector", par)
            for u in range(NU):
                stt(b, "vector", "evR", xo[par][:, u * 512:(u + 1) * 512],
                    p_y[2 * par + u][:, :], b3s[:, oc:oc + 1],
                    xre[par][:, u * 512:(u + 1) * 512], ALU.add, ALU.add)
            r_py.release(par, "evR", b.val("evR"))
            r_xre.release(par, "evR", b.val("evR"))
            wait(b, "scalar", "evR")
            dma(b, "scalar", "ost", x_out[oc * 128:(oc + 1) * 128, tok0:tok0 + TG],
                xo[par][:, :])
            r_xo.release(par, "ost", b.val("ost"))
    wait(b, "scalar", "ost")
    return b.finish()



def build_attn_a(debug=False):
    b = B("attn_a")
    b.constf(EPS_LN)
    nc = b.nc
    x_in = nc.dram_tensor("x_in", [D, T], F32, kind="ExternalInput").ap()
    wqd = nc.dram_tensor("wqd", [16, 128, 16, 128], F32, kind="ExternalInput").ap()
    wkd = nc.dram_tensor("wkd", [16, 128, 16, 128], F32, kind="ExternalInput").ap()
    wvr = nc.dram_tensor("wvr", [16, 128, 2048], F32, kind="ExternalInput").ap()
    wod = nc.dram_tensor("wod", [16, 128, 16, 128], F32, kind="ExternalInput").ap()
    bqm = nc.dram_tensor("bqm", [128, 16], F32, kind="ExternalInput").ap()
    bkm = nc.dram_tensor("bkm", [128, 16], F32, kind="ExternalInput").ap()
    bom = nc.dram_tensor("bom", [128, 16], F32, kind="ExternalInput").ap()
    bvb = nc.dram_tensor("bvb", [128, 2048], F32, kind="ExternalInput").ap()
    x_out = nc.dram_tensor("x_out", [D, T], F32, kind="ExternalOutput").ap()
    qf = nc.dram_tensor("qf", [NB, D, S], BF16, kind="Internal").ap()
    kf = nc.dram_tensor("kf", [NB, D, S], BF16, kind="Internal").ap()
    vt = nc.dram_tensor("vt", [NB, S, D], BF16, kind="Internal").ap()
    if debug:
        q_dbg = nc.dram_tensor("q_dbg", [NB, D, S], BF16, kind="ExternalOutput").ap()
        v_dbg = nc.dram_tensor("v_dbg", [NB, S, D], BF16, kind="ExternalOutput").ap()
        a_dbg = nc.dram_tensor("a_dbg", [D, T], BF16, kind="ExternalOutput").ap()

    ti = make_ln_tiles(b, 512, with_xln=False)
    big = b.sb("big", [128, KC, S], BF16)
    wb = [b.sb(f"wbA{i}", [128, 16, 128], BF16) for i in range(2)]
    wvs = b.sb("wvs", [128, KC, 512], BF16)
    qh = b.sb("qh", [128, S], BF16)
    kh = b.sb("kh", [128, S], BF16)
    vh = b.sb("vh", [128, KC, 128], BF16)
    Et = [b.sb(f"Et{i}", [128, 512], BF16) for i in range(2)]
    masks = b.sb("masks", [128, 4, 512], BF16)
    zi = b.sb("zi", [128, 512], F32)
    qo = [b.sb(f"qo{i}", [128, 512], BF16) for i in range(2)]
    bq_s = b.sb("bq_s", [128, 16], F32)
    bk_s = b.sb("bk_s", [128, 16], F32)
    bo_s = b.sb("bo_s", [128, 16], F32)
    bvr = b.sb("bvr", [128, 2048], F32)
    xre = [b.sb(f"xreA{i}", [128, 512], F32) for i in range(2)]
    xo = [b.sb(f"xoA{i}", [128, 512], F32) for i in range(2)]
    p_y = [b.ps(f"p_yA{i}", [128, 512]) for i in range(2)]
    pS = [b.ps(f"pS{i}", [128, 512]) for i in range(2)]
    pZ = b.ps("pZ", [128, 512])
    pO = b.ps("pO", [128, 512])

    r_wb, r_py, r_ps, r_et, r_qo, r_xre, r_xo = (Rot(2) for _ in range(7))
    r_pz, r_po = Rot(1), Rot(1)
    r_hbuf = Rot(1)
    r_wvs = Rot(1)

    v_b = dma(b, "sync", "xld", bq_s[:, :], bqm)
    v_b = dma(b, "sync", "xld", bk_s[:, :], bkm)
    v_b = dma(b, "sync", "xld", bo_s[:, :], bom)
    v_b = dma(b, "sync", "xld", bvr[:, :], bvb)
    wait(b, "scalar", "xld", v_b)
    wait(b, "vector", "xld", v_b)
    memset(b, "gpsimd", "ini", ti["ones"][:, :], 1.0)
    memset(b, "gpsimd", "ini", masks[:, :, :], 1.0)
    wait(b, "gpsimd", "ini")
    for i in range(4):
        h, v = b.inc("ini")
        b.on("gpsimd", lambda q, i=i, h=h: q.affine_select(
            out=masks[:, i, :], in_=masks[:, i, :], pattern=[[1, 512]],
            compare_op=ALU.is_ge, fill=0.0, base=-(i * 128),
            channel_multiplier=-1).then_inc(h, 1))
    wait(b, "vector", "ini")
    wait(b, "tensor", "ini")

    SCALE = 1.0 / np.sqrt(128.0)

    for b_i in range(NB):
        s0 = b_i * S
        for grp in range(4):
            emit_ln(b, ti, x_in, s0 + grp * 512, 512,
                    dst=big[:, :, grp * 512:(grp + 1) * 512],
                    dst_guard=("mmw", b.val("mmw")))

        for wsrc, bias_s, dst_h in ((wqd, bq_s, qf), (wkd, bk_s, kf)):
            for oc in range(16):
                par = oc % 2
                r_wb.acquire(b, "gpsimd", par)
                vw = dma(b, "gpsimd", "wld", wb[par][:, :, :], wsrc[oc])
                wait(b, "tensor", "wld", vw)
                for tq in range(4):
                    pr = (oc * 4 + tq) % 2
                    r_py.acquire(b, "tensor", pr)
                    for kc in range(KC):
                        mm(b, "mmq", p_y[pr][:, :], wb[par][:, kc, :],
                           big[:, kc, tq * 512:(tq + 1) * 512],
                           start=(kc == 0), stop=(kc == KC - 1), inc=(kc == KC - 1))
                    wait(b, "scalar", "mmq")
                    er = (oc * 4 + tq) % 2
                    r_qo.acquire(b, "scalar", er)
                    act(b, "evQ", qo[er][:, :], p_y[pr][:, :], AF.Identity,
                        bias=bias_s[:, oc:oc + 1])
                    r_py.release(pr, "evQ", b.val("evQ"))
                    wait(b, "scalar", "evQ")
                    dma(b, "scalar", "qst",
                        dst_h[b_i, oc * 128:(oc + 1) * 128, tq * 512:(tq + 1) * 512],
                        qo[er][:, :])
                    r_qo.release(er, "qst", b.val("qst"))
                r_wb.release(par, "mmq", b.val("mmq"))

        for ds in range(4):
            r_wvs.acquire(b, "gpsimd", 0)
            for kc in range(KC):
                dma(b, "gpsimd", "wld", wvs[:, kc, :],
                    wvr[kc, :, ds * 512:(ds + 1) * 512])
            vw = b.val("wld")
            wait(b, "tensor", "wld", vw)
            for tc in range(16):
                pr = (ds * 16 + tc) % 2
                r_py.acquire(b, "tensor", pr)
                for kc in range(KC):
                    mm(b, "mmq", p_y[pr][:, :], big[:, kc, tc * 128:(tc + 1) * 128],
                       wvs[:, kc, :],
                       start=(kc == 0), stop=(kc == KC - 1), inc=(kc == KC - 1))
                wait(b, "vector", "mmq")
                er = (ds * 16 + tc) % 2
                r_qo.acquire(b, "vector", er)
                tt(b, "vector", "evQ", qo[er][:, :], p_y[pr][:, :],
                   bvr[:, ds * 512:(ds + 1) * 512], ALU.add)
                r_py.release(pr, "evQ", b.val("evQ"))
                wait(b, "scalar", "evQ")
                dma(b, "scalar", "qst",
                    vt[b_i, tc * 128:(tc + 1) * 128, ds * 512:(ds + 1) * 512],
                    qo[er][:, :])
                r_qo.release(er, "qst", b.val("qst"))
            r_wvs.release(0, "mmq", b.val("mmq"))

        v_projdone = b.val("mmq")
        wait(b, "vector", "mmq", v_projdone)

        wait(b, "sync", "qst", b.val("qst"))
        for hh in range(16):
            r_hbuf.acquire(b, "sync", 0)
            dma(b, "sync", "hld", qh[:, :], qf[b_i, hh * 128:(hh + 1) * 128, :])
            dma(b, "sync", "hld", kh[:, :], kf[b_i, hh * 128:(hh + 1) * 128, :])
            v_h = dma(b, "sync", "hld",
                      vh[:, :, :],
                      vt[b_i].rearrange("(c p) d -> p c d", p=128)[:, :, hh * 128:(hh + 1) * 128])
            wait(b, "tensor", "hld", v_h)
            wait(b, "scalar", "hld", v_h)
            for qt in range(4):
                nkc = 4 * (qt + 1)
                r_pz.acquire(b, "tensor", 0)
                r_po.acquire(b, "tensor", 0)
                for kc in range(nkc):
                    pr = kc % 2
                    r_ps.acquire(b, "tensor", pr)
                    mm(b, "mms", pS[pr][:, :], kh[:, kc * 128:(kc + 1) * 128],
                       qh[:, qt * 512:(qt + 1) * 512], start=True, stop=True, inc=True)
                    wait(b, "scalar", "mms")
                    er = kc % 2
                    r_et.acquire(b, "scalar", er)
                    act(b, "evE", Et[er][:, :], pS[pr][:, :], AF.Exp, scale=SCALE)
                    r_ps.release(pr, "evE", b.val("evE"))
                    if kc >= 4 * qt:
                        wait(b, "vector", "evE")
                        tt(b, "vector", "evE", Et[er][:, :], Et[er][:, :],
                           masks[:, kc - 4 * qt, :], ALU.mult)
                    wait(b, "tensor", "evE")
                    mm(b, None, pZ[:, :], ti["ones"][:, :], Et[er][:, :],
                       start=(kc == 0), stop=(kc == nkc - 1))
                    mm(b, "mma", pO[:, :], vh[:, kc, :], Et[er][:, :],
                       start=(kc == 0), stop=(kc == nkc - 1), inc=True)
                    r_et.release(er, "mma", b.val("mma"))
                wait(b, "vector", "mma")
                recip(b, "vector", "evA", zi[:, :], pZ[:, :])
                wait(b, "vector", "evA")
                tt(b, "vector", "evA", big[:, hh, qt * 512:(qt + 1) * 512],
                   pO[:, :], zi[:, :], ALU.mult)
                v_evA = b.val("evA")
                r_pz.release(0, "evA", v_evA)
                r_po.release(0, "evA", v_evA)
            r_hbuf.release(0, "mma", b.val("mma"))
        if debug:
            wait(b, "scalar", "evA")
            for c in range(KC):
                dma(b, "scalar", "qst", a_dbg[c * 128:(c + 1) * 128, s0:s0 + S],
                    big[:, c, :])
            wait(b, "vector", "qst", b.val("qst"))

        wait(b, "tensor", "evA")
        for oc in range(16):
            par = oc % 2
            r_wb.acquire(b, "gpsimd", par)
            vw = dma(b, "gpsimd", "wld", wb[par][:, :, :], wod[oc])
            wait(b, "tensor", "wld", vw)
            for tt_i in range(4):
                pr = (oc * 4 + tt_i) % 2
                xr = (oc * 4 + tt_i) % 2
                r_xre.acquire(b, "sync", xr)
                vx = dma(b, "sync", "xld", xre[xr][:, :],
                         x_in[oc * 128:(oc + 1) * 128,
                              s0 + tt_i * 512:s0 + (tt_i + 1) * 512])
                r_py.acquire(b, "tensor", pr)
                for kc in range(KC):
                    mm(b, "mmw", p_y[pr][:, :], wb[par][:, kc, :],
                       big[:, kc, tt_i * 512:(tt_i + 1) * 512],
                       start=(kc == 0), stop=(kc == KC - 1), inc=(kc == KC - 1))
                wait(b, "vector", "mmw")
                wait(b, "vector", "xld", vx)
                xw = (oc * 4 + tt_i) % 2
                r_xo.acquire(b, "vector", xw)
                stt(b, "vector", "evR", xo[xw][:, :], p_y[pr][:, :],
                    bo_s[:, oc:oc + 1], xre[xr][:, :], ALU.add, ALU.add)
                v_r = b.val("evR")
                r_py.release(pr, "evR", v_r)
                r_xre.release(xr, "evR", v_r)
                wait(b, "scalar", "evR")
                dma(b, "scalar", "ost",
                    x_out[oc * 128:(oc + 1) * 128, s0 + tt_i * 512:s0 + (tt_i + 1) * 512],
                    xo[xw][:, :])
                r_xo.release(xw, "ost", b.val("ost"))
                wait(b, "gpsimd", "ost", b.val("ost"))
                wait(b, "sync", "ost", b.val("ost"))
            r_wb.release(par, "mmw", b.val("mmw"))
    if debug:
        wait(b, "gpsimd", "qst")
        dma(b, "gpsimd", "ost", q_dbg, qf)
        dma(b, "gpsimd", "ost", v_dbg, vt)
    wait(b, "scalar", "ost")
    wait(b, "gpsimd", "ost")
    return b.finish()



def build_attn_b(debug=False):
    b = B("attn_b")
    b.constf(EPS_LN)
    nc = b.nc
    x_in = nc.dram_tensor("x_in", [D, T], F32, kind="ExternalInput").ap()
    wqd = nc.dram_tensor("wqd", [16, 128, 16, 128], F32, kind="ExternalInput").ap()
    wkr = nc.dram_tensor("wkr", [16, 128, 512], F32, kind="ExternalInput").ap()
    wvr2 = nc.dram_tensor("wvr2", [16, 128, 512], F32, kind="ExternalInput").ap()
    wod = nc.dram_tensor("wod", [16, 128, 16, 128], F32, kind="ExternalInput").ap()
    bqm = nc.dram_tensor("bqm", [128, 16], F32, kind="ExternalInput").ap()
    bkb = nc.dram_tensor("bkb", [128, 512], F32, kind="ExternalInput").ap()
    bvb = nc.dram_tensor("bvb", [128, 512], F32, kind="ExternalInput").ap()
    x_out = nc.dram_tensor("x_out", [D, T], F32, kind="ExternalOutput").ap()
    sqf = nc.dram_tensor("sqf", [NB, D, S], BF16, kind="Internal").ap()
    if debug:
        sq_dbg = nc.dram_tensor("sq_dbg", [NB, D, S], BF16, kind="ExternalOutput").ap()
        qraw_dbg = nc.dram_tensor("qraw_dbg", [NB, D, S], BF16, kind="ExternalOutput").ap()
        xlnb_dbg = nc.dram_tensor("xlnb_dbg", [D, T], BF16, kind="ExternalOutput").ap()
        a_dbg = nc.dram_tensor("a_dbg", [D, T], BF16, kind="ExternalOutput").ap()

    ti = make_ln_tiles(b, 512, with_xln=False)
    big = b.sb("big", [128, KC, S], BF16)
    wb = [b.sb(f"wbB{i}", [128, 16, 128], BF16) for i in range(2)]
    wks = b.sb("wks", [128, KC, 512], BF16)
    wvs = b.sb("wvs", [128, KC, 512], BF16)
    sk_tm = b.sb("sk_tm", [128, 16, 512], BF16)
    sv_tm = b.sb("sv_tm", [128, 16, 512], BF16)
    kvm = b.sb("kvm", [128, 4, 128], BF16)
    ksr = b.sb("ksr", [128, 4, 128], BF16)
    sqh = b.sb("sqh", [128, S], BF16)
    tmp1 = b.sb("tmpB1", [128, 512], F32)
    tmp2 = b.sb("tmpB2", [128, 512], F32)
    tmp3 = b.sb("tmpB3", [128, 512], F32)
    ni = b.sb("niB", [128, 512], F32)
    qo2_dbg = b.sb("qo2_dbg", [128, 512], BF16)
    qo = [b.sb(f"qoB{i}", [128, 512], BF16) for i in range(2)]
    bq_s = b.sb("bq_sB", [128, 16], F32)
    nbq_s = b.sb("nbq_sB", [128, 16], F32)
    bk_s = b.sb("bk_sB", [128, 512], F32)
    bv_s = b.sb("bv_sB", [128, 512], F32)
    xre = [b.sb(f"xreB{i}", [128, 512], F32) for i in range(2)]
    xo = [b.sb(f"xoB{i}", [128, 512], F32) for i in range(2)]
    p_y = [b.ps(f"p_yB{i}", [128, 512]) for i in range(2)]
    pkv = b.ps("pkv", [128, 128])
    pks = b.ps("pks", [128, 4])
    pN = b.ps("pN", [128, 512])
    pQk = b.ps("pQk", [128, 512])

    r_wb, r_py, r_qo, r_xre, r_xo = (Rot(2) for _ in range(5))
    r_sqh = Rot(1)
    r_pn, r_pqk = Rot(1), Rot(1)
    r_tmp = Rot(1)

    v_b = dma(b, "sync", "xld", bq_s[:, :], bqm)
    v_b = dma(b, "sync", "xld", bk_s[:, :], bkb)
    v_b = dma(b, "sync", "xld", bv_s[:, :], bvb)
    for kc in range(KC):
        dma(b, "gpsimd", "wld", wks[:, kc, :], wkr[kc])
        dma(b, "gpsimd", "wld", wvs[:, kc, :], wvr2[kc])
    v_wkv = b.val("wld")
    memset(b, "gpsimd", "ini", ti["ones"][:, :], 1.0)
    wait(b, "tensor", "ini")
    wait(b, "scalar", "xld", v_b)
    wait(b, "vector", "xld", v_b)
    tscalar(b, "vector", "ini", nbq_s[:, :], bq_s[:, :], -1.0, None, ALU.mult)
    wait(b, "scalar", "ini")

    for b_i in range(NB):
        s0 = b_i * S
        for grp in range(4):
            emit_ln(b, ti, x_in, s0 + grp * 512, 512,
                    dst=big[:, :, grp * 512:(grp + 1) * 512],
                    dst_guard=("mmw", b.val("mmw")))

        if debug:
            wait(b, "scalar", "ev", ti["xb_free"])
            for c in range(KC):
                dma(b, "scalar", "qst", xlnb_dbg[c * 128:(c + 1) * 128, s0:s0 + S],
                    big[:, c, :])
        for oc in range(16):
            par = oc % 2
            r_wb.acquire(b, "gpsimd", par)
            vw = dma(b, "gpsimd", "wld", wb[par][:, :, :], wqd[oc])
            wait(b, "tensor", "wld", vw)
            for tq in range(4):
                pr = (oc * 4 + tq) % 2
                r_py.acquire(b, "tensor", pr)
                for kc in range(KC):
                    mm(b, "mmq", p_y[pr][:, :], wb[par][:, kc, :],
                       big[:, kc, tq * 512:(tq + 1) * 512],
                       start=(kc == 0), stop=(kc == KC - 1), inc=(kc == KC - 1))
                wait(b, "scalar", "mmq")
                r_tmp.acquire(b, "scalar", 0)
                act(b, "evQ", tmp1[:, :], p_y[pr][:, :], AF.Relu, bias=bq_s[:, oc:oc + 1])
                act(b, "evQ", tmp2[:, :], p_y[pr][:, :], AF.Relu, scale=-1.0,
                    bias=nbq_s[:, oc:oc + 1])
                if debug:
                    wait(b, "scalar", "qst", b.val("qst"))
                    act(b, "evQ", qo2_dbg[:, :], p_y[pr][:, :], AF.Identity,
                        bias=bq_s[:, oc:oc + 1])
                r_py.release(pr, "evQ", b.val("evQ"))
                wait(b, "scalar", "evQ")
                act(b, "evQ", tmp2[:, :], tmp2[:, :], AF.Exp, scale=-1.0)
                if debug:
                    wait(b, "scalar", "evQ")
                    dma(b, "scalar", "qst",
                        qraw_dbg[b_i, oc * 128:(oc + 1) * 128, tq * 512:(tq + 1) * 512],
                        qo2_dbg[:, :])
                wait(b, "vector", "evQ")
                er = (oc * 4 + tq) % 2
                r_qo.acquire(b, "vector", er)
                tt(b, "vector", "evV", qo[er][:, :], tmp1[:, :], tmp2[:, :], ALU.add)
                r_tmp.release(0, "evV", b.val("evV"))
                wait(b, "scalar", "evV")
                dma(b, "scalar", "qst",
                    sqf[b_i, oc * 128:(oc + 1) * 128, tq * 512:(tq + 1) * 512],
                    qo[er][:, :])
                r_qo.release(er, "qst", b.val("qst"))
            r_wb.release(par, "mmq", b.val("mmq"))

        wait(b, "tensor", "wld", v_wkv)
        wait(b, "vector", "mmk", b.val("mmk"))
        for tc in range(16):
            r_pqk.acquire(b, "tensor", 0)
            for kc in range(KC):
                mm(b, "mmq", pQk[:, :], big[:, kc, tc * 128:(tc + 1) * 128],
                   wks[:, kc, :], start=(kc == 0), stop=(kc == KC - 1),
                   inc=(kc == KC - 1))
            wait(b, "vector", "mmq")
            r_tmp.acquire(b, "vector", 0)
            tt(b, "vector", "evK", tmp1[:, :], pQk[:, :], bk_s[:, :], ALU.add)
            wait(b, "scalar", "evK")
            act(b, "evK", tmp2[:, :], tmp1[:, :], AF.Relu)
            act(b, "evK", tmp3[:, :], tmp1[:, :], AF.Relu, scale=-1.0)
            wait(b, "scalar", "evK")
            act(b, "evK", tmp3[:, :], tmp3[:, :], AF.Exp, scale=-1.0)
            wait(b, "vector", "evK")
            tt(b, "vector", "evK", sk_tm[:, tc, :], tmp2[:, :], tmp3[:, :], ALU.add)
            r_tmp.release(0, "evK", b.val("evK"))
            r_pqk.release(0, "evK", b.val("evK"))
            r_pn.acquire(b, "tensor", 0)
            for kc in range(KC):
                mm(b, "mmq", pN[:, :], big[:, kc, tc * 128:(tc + 1) * 128],
                   wvs[:, kc, :], start=(kc == 0), stop=(kc == KC - 1),
                   inc=(kc == KC - 1))
            wait(b, "vector", "mmq")
            tt(b, "vector", "evK", sv_tm[:, tc, :], pN[:, :], bv_s[:, :], ALU.add)
            r_pn.release(0, "evK", b.val("evK"))

        v_kv = b.val("evK")
        wait(b, "tensor", "evK", v_kv)
        wait(b, "vector", "mmk", b.val("mmk"))
        for g in range(4):
            gs = slice(g * 128, (g + 1) * 128)
            if g > 0:
                wait(b, "tensor", "evM", b.val("evM"))
            for tc in range(16):
                mm(b, "mmk", pkv[:, :], sk_tm[:, tc, gs], sv_tm[:, tc, gs],
                   start=(tc == 0), stop=(tc == 15), inc=(tc == 15))
            for tc in range(16):
                mm(b, "mmk", pks[:, g:g + 1], sk_tm[:, tc, gs], ti["ones"][:, 0:1],
                   start=(tc == 0), stop=(tc == 15), inc=(tc == 15))
            wait(b, "vector", "mmk")
            tcopy(b, "vector", "evM", kvm[:, g, :], pkv[:, :])
            ks1 = pks[:, g:g + 1]
            ksb = bass.AP(ks1.tensor, ks1.offset, [ks1.ap[0], [0, 128]])
            tcopy(b, "vector", "evM", ksr[:, g, :], ksb)
        v_evM = b.val("evM")
        wait(b, "tensor", "evM", v_evM)

        wait(b, "vector", "mmq")
        wait(b, "sync", "qst", b.val("qst"))
        for hh in range(16):
            g = hh // 4
            r_sqh.acquire(b, "sync", 0)
            v_h = dma(b, "sync", "hld", sqh[:, :], sqf[b_i, hh * 128:(hh + 1) * 128, :])
            wait(b, "tensor", "hld", v_h)
            for tq in range(4):
                sl = slice(tq * 512, (tq + 1) * 512)
                r_pn.acquire(b, "tensor", 0)
                mm(b, "mmk", pN[:, :], ksr[:, g, :], sqh[:, sl], start=True,
                   stop=True, inc=True)
                r_pqk.acquire(b, "tensor", 0)
                mm(b, "mmk", pQk[:, :], kvm[:, g, :], sqh[:, sl], start=True,
                   stop=True, inc=True)
                wait(b, "vector", "mmk")
                tscalar(b, "vector", "evA", ni[:, :], pN[:, :], 1e-6, None, ALU.add)
                r_pn.release(0, "evA", b.val("evA"))
                wait(b, "vector", "evA")
                recip(b, "vector", "evA", ni[:, :], ni[:, :])
                wait(b, "vector", "evA")
                tt(b, "vector", "evA", big[:, hh, sl], pQk[:, :], ni[:, :], ALU.mult)
                r_pqk.release(0, "evA", b.val("evA"))
            r_sqh.release(0, "mmk", b.val("mmk"))
        if debug:
            wait(b, "scalar", "evA")
            for c in range(KC):
                dma(b, "scalar", "qst", a_dbg[c * 128:(c + 1) * 128, s0:s0 + S],
                    big[:, c, :])
            wait(b, "vector", "qst", b.val("qst"))

        wait(b, "tensor", "evA")
        for oc in range(16):
            par = oc % 2
            r_wb.acquire(b, "gpsimd", par)
            vw = dma(b, "gpsimd", "wld", wb[par][:, :, :], wod[oc])
            wait(b, "tensor", "wld", vw)
            for tt_i in range(4):
                pr = (oc * 4 + tt_i) % 2
                xr = (oc * 4 + tt_i) % 2
                r_xre.acquire(b, "sync", xr)
                vx = dma(b, "sync", "xld", xre[xr][:, :],
                         x_in[oc * 128:(oc + 1) * 128,
                              s0 + tt_i * 512:s0 + (tt_i + 1) * 512])
                r_py.acquire(b, "tensor", pr)
                for kc in range(KC):
                    mm(b, "mmw", p_y[pr][:, :], wb[par][:, kc, :],
                       big[:, kc, tt_i * 512:(tt_i + 1) * 512],
                       start=(kc == 0), stop=(kc == KC - 1), inc=(kc == KC - 1))
                wait(b, "vector", "mmw")
                wait(b, "vector", "xld", vx)
                xw = (oc * 4 + tt_i) % 2
                r_xo.acquire(b, "vector", xw)
                tt(b, "vector", "evR", xo[xw][:, :], p_y[pr][:, :], xre[xr][:, :],
                   ALU.add)
                v_r = b.val("evR")
                r_py.release(pr, "evR", v_r)
                r_xre.release(xr, "evR", v_r)
                wait(b, "scalar", "evR")
                dma(b, "scalar", "ost",
                    x_out[oc * 128:(oc + 1) * 128, s0 + tt_i * 512:s0 + (tt_i + 1) * 512],
                    xo[xw][:, :])
                r_xo.release(xw, "ost", b.val("ost"))
                wait(b, "gpsimd", "ost", b.val("ost"))
                wait(b, "sync", "ost", b.val("ost"))

            r_wb.release(par, "mmw", b.val("mmw"))
    if debug:
        wait(b, "gpsimd", "qst")
        dma(b, "gpsimd", "ost", sq_dbg, sqf)
    wait(b, "scalar", "ost")
    wait(b, "gpsimd", "ost")
    return b.finish()



def build_moe_dense(debug=False):
    b = B("moe")
    b.constf(EPS_LN)
    nc = b.nc
    x_in = nc.dram_tensor("x_in", [D, T], F32, kind="ExternalInput").ap()
    rtr = nc.dram_tensor("rtr", [16, 128, 8], F32, kind="ExternalInput").ap()
    w1d = nc.dram_tensor("w1d", [8, 22, 128, 16, 128], F32, kind="ExternalInput").ap()
    w2d = nc.dram_tensor("w2d", [8, 16, 128, 11, 128], F32, kind="ExternalInput").ap()
    x_out = nc.dram_tensor("x_out", [D, T], F32, kind="ExternalOutput").ap()
    if debug:
        g_dbg = nc.dram_tensor("g_dbg", [8, T], BF16, kind="ExternalOutput").ap()
        l_dbg = nc.dram_tensor("l_dbg", [T, 8], F32, kind="ExternalOutput").ap()
        lr_dbg = nc.dram_tensor("lr_dbg", [T, 8], F32, kind="ExternalOutput").ap()
        ms_dbg = nc.dram_tensor("ms_dbg", [T, 2], F32, kind="ExternalOutput").ap()
        ya_dbg = nc.dram_tensor("ya_dbg", [11 * 128, 8, T], BF16, kind="ExternalOutput").ap()

    TG = 512
    NG = T // TG

    ti = make_ln_tiles(b, TG)
    xln = ti["xln"]
    rts32 = b.sb("rts32", [128, 16, 8], F32)
    onesf = b.sb("onesfM", [128, 128], F32)
    nrsum = b.sb("nrsumM", [128, 8], F32)
    xt32 = b.sb("xt32M", [128, 16, 128], F32)
    mu_t = b.sb("mu_tM", [128, 1], F32)
    rstd_t = b.sb("rstd_tM", [128, 1], F32)
    Ltmp = b.sb("LtmpM", [128, 8], F32)
    idn = b.sb("idnM", [128, 128], BF16)
    Lsb = b.sb("Lsb", [128, 8], F32)
    L2 = b.sb("L2M", [128, 8], F32)
    m1 = b.sb("m1M", [128, 1], F32)
    m2 = b.sb("m2M", [128, 1], F32)
    w1g = b.sb("w1gM", [128, 1], F32)
    w2g = b.sb("w2gM", [128, 1], F32)
    eq1 = b.sb("eq1M", [128, 8], F32)
    eq2 = b.sb("eq2M", [128, 8], F32)
    Gb = b.sb("GbM", [128, 128], BF16)
    gem = b.sb("gemM", [8, TG], BF16)
    sel = b.sb("selM", [8, 8 * 128], BF16)
    grep = b.sb("grepM", [128, 8, TG], BF16)
    sg = b.sb("sgM", [128, TG], BF16)
    t1 = b.sb("t1M", [128, TG], BF16)
    yact = b.sb("yactM", [128, 88, TG], BF16)
    wb1 = [b.sb(f"wb1M{i}", [128, 16, 128], BF16) for i in range(2)]
    wb2 = [b.sb(f"wb2M{i}", [128, 11, 128], BF16) for i in range(2)]
    xre = [b.sb(f"xreM{i}", [128, TG], F32) for i in range(2)]
    xo = [b.sb(f"xoM{i}", [128, TG], F32) for i in range(2)]
    p_A = b.ps("p_AM", [128, 512])
    p_T = b.ps("p_TM", [128, 128], BF16)
    p_g = b.ps("p_gM", [128, 512])
    p_u = b.ps("p_uM", [128, 512])
    p_y2 = [b.ps(f"p_y2M{i}", [128, 512]) for i in range(2)]

    r_wb1, r_wb2 = Rot(2), Rot(2)
    r_pA, r_pg, r_pu, r_pT = Rot(1), Rot(1), Rot(1), Rot(1)
    r_py2 = Rot(2)
    r_xre, r_xo = Rot(2), Rot(2)
    r_gch = Rot(1)
    r_x32 = Rot(1)
    r_sg = Rot(1)
    r_yact = Rot(1)

    for kc in range(KC):
        dma(b, "sync", "xld", rts32[:, kc, :], rtr[kc])
    memset(b, "gpsimd", "ini", onesf[:, :], 1.0)
    memset(b, "gpsimd", "ini", ti["ones"][:, :], 1.0)
    memset(b, "gpsimd", "ini", idn[:, :], 0.0)
    wait(b, "gpsimd", "ini")
    h, _ = b.inc("ini")
    b.on("gpsimd", lambda q, h=h: q.affine_select(
        out=idn[:, :], in_=idn[:, :], compare_op=ALU.not_equal, fill=1.0,
        base=0, pattern=[[-1, 128]], channel_multiplier=1).then_inc(h, 1))
    memset(b, "gpsimd", "ini", sel[:, :], 1.0)
    memset(b, "gpsimd", "ini", Gb[:, :], 0.0)
    wait(b, "gpsimd", "ini")
    h, _ = b.inc("ini")
    b.on("gpsimd", lambda q, h=h: q.affine_select(
        out=sel[:, :].rearrange("p (e m) -> p e m", e=8),
        in_=sel[:, :].rearrange("p (e m) -> p e m", e=8),
        pattern=[[-1, 8], [0, 128]], compare_op=ALU.is_equal, fill=0.0,
        channel_multiplier=1).then_inc(h, 1))
    wait(b, "tensor", "ini")
    wait(b, "vector", "ini")
    wait(b, "tensor", "xld", b.val("xld"))
    for kc in range(KC):
        mm(b, "mmq", p_A[:, 0:8], onesf[:, :], rts32[:, kc, :],
           start=(kc == 0), stop=(kc == KC - 1), inc=(kc == KC - 1))
    wait(b, "scalar", "mmq")
    act(b, "ev2", nrsum[:, :], p_A[:, 0:8], AF.Identity, scale=-1.0)
    wait(b, "vector", "ev2")

    for g in range(NG):
        tok0 = g * TG
        emit_ln(b, ti, x_in, tok0, TG, dst_guard=("mmw", b.val("mmw")))

        for tc in range(4):
            tsl = slice(tc * 128, (tc + 1) * 128)
            r_x32.acquire(b, "sync", 0)
            for kc in range(KC):
                dma(b, "sync", "xld", xt32[:, kc, :],
                    x_in[kc * 128:(kc + 1) * 128, tok0 + tc * 128:tok0 + (tc + 1) * 128])
            v_x32 = b.val("xld")
            wait(b, "tensor", "xld", v_x32)
            r_pA.acquire(b, "tensor", 0)
            for kc in range(KC):
                mm(b, "mmq", p_A[:, 0:8], xt32[:, kc, :], rts32[:, kc, :],
                   start=(kc == 0), stop=(kc == KC - 1), inc=(kc == KC - 1))
            r_x32.release(0, "mmq", b.val("mmq"))
            r_pg.acquire(b, "tensor", 0)
            mm(b, "mmq", p_g[:, 0:1], ti["mu"][0:1, tsl], onesf[0:1, 0:1],
               start=True, stop=True)
            mm(b, "mmq", p_g[:, 1:2], ti["var"][0:1, tsl], onesf[0:1, 0:1],
               start=True, stop=True, inc=True)
            wait(b, "vector", "mmq")
            r_gch.acquire(b, "vector", 0)
            tcopy(b, "vector", "ev2", mu_t[:, :], p_g[:, 0:1])
            tcopy(b, "vector", "ev2", rstd_t[:, :], p_g[:, 1:2])
            r_pg.release(0, "ev2", b.val("ev2"))
            wait(b, "vector", "ev2")
            if debug:
                wait(b, "scalar", "ev2", b.val("ev2"))
                act(b, "ev2", Ltmp[:, :], p_A[:, 0:8], AF.Identity)
                wait(b, "scalar", "ev2")
                dma(b, "scalar", "ost", lr_dbg[tok0 + tc * 128:tok0 + (tc + 1) * 128, :],
                    Ltmp[:, :])
                wait(b, "vector", "ost", b.val("ost"))
            stt(b, "vector", "ev2", Ltmp[:, :], nrsum[:, :], mu_t[:, :],
                p_A[:, 0:8], ALU.mult, ALU.add)
            r_pA.release(0, "ev2", b.val("ev2"))
            wait(b, "vector", "ev2")
            tscalar(b, "vector", "ev2", Lsb[:, :], Ltmp[:, :], rstd_t[:, :], None,
                    ALU.mult)
            wait(b, "vector", "ev2")
            if debug:
                wait(b, "scalar", "ev2", b.val("ev2"))
                dma(b, "scalar", "ost", l_dbg[tok0 + tc * 128:tok0 + (tc + 1) * 128, :],
                    Lsb[:, :])
                dma(b, "scalar", "ost", ms_dbg[tok0 + tc * 128:tok0 + (tc + 1) * 128, 0:1],
                    mu_t[:, :], allow_slow_non_contiguous=True)
                dma(b, "scalar", "ost", ms_dbg[tok0 + tc * 128:tok0 + (tc + 1) * 128, 1:2],
                    rstd_t[:, :], allow_slow_non_contiguous=True)
                wait(b, "vector", "ost", b.val("ost"))
            h2, _ = b.inc("ev2")
            b.on("vector", lambda q, h=h2: q.reduce_max(
                m1[:, :], Lsb[:, :], axis=AX.X).then_inc(h, 1))
            wait(b, "vector", "ev2")
            tscalar(b, "vector", "ev2", eq1[:, :], Lsb[:, :], m1[:, :], None, ALU.is_equal)
            wait(b, "vector", "ev2")
            stt(b, "vector", "ev2", L2[:, :], eq1[:, :], -1e30, Lsb[:, :],
                ALU.mult, ALU.add)
            wait(b, "vector", "ev2")
            h2, _ = b.inc("ev2")
            b.on("vector", lambda q, h=h2: q.reduce_max(
                m2[:, :], L2[:, :], axis=AX.X).then_inc(h, 1))
            wait(b, "vector", "ev2")
            tscalar(b, "vector", "ev2", eq2[:, :], L2[:, :], m2[:, :], None, ALU.is_equal)
            tt(b, "vector", "ev2", w1g[:, :], m2[:, :], m1[:, :], ALU.subtract)
            wait(b, "scalar", "ev2")
            act(b, "ev2", w1g[:, :], w1g[:, :], AF.Sigmoid, scale=-1.0)
            wait(b, "vector", "ev2")
            tscalar(b, "vector", "ev2", w2g[:, :], w1g[:, :], -1.0, 1.0,
                    ALU.mult, ALU.add)
            tscalar(b, "vector", "ev2", eq1[:, :], eq1[:, :], w1g[:, :], None, ALU.mult)
            wait(b, "vector", "ev2")
            stt(b, "vector", "ev2", Gb[:, 0:8], eq2[:, :], w2g[:, :], eq1[:, :],
                ALU.mult, ALU.add)
            v_g = b.val("ev2")
            wait(b, "tensor", "ev2", v_g)
            r_pT.acquire(b, "tensor", 0)
            h2, _ = b.inc("mmq")
            b.on("tensor", lambda q, h=h2: q.transpose(
                p_T[:, :], Gb[:, :], idn[:, :]).then_inc(h, 1))
            wait(b, "vector", "mmq")
            tcopy(b, "vector", "ev2", gem[0:8, tsl], p_T[0:8, 0:128])
            r_pT.release(0, "ev2", b.val("ev2"))
            r_gch.release(0, "ev2", b.val("ev2"))
        ti["stats_guard"] = ("mmq", b.val("mmq"))
        v_gem = b.val("ev2")
        wait(b, "tensor", "ev2", v_gem)
        for e in range(8):
            r_pA.acquire(b, "tensor", 0)
            mm(b, "mmq", p_A[:, :], sel[:, e * 128:(e + 1) * 128], gem[0:8, :],
               start=True, stop=True, inc=True)
            wait(b, "vector", "mmq")
            tcopy(b, "vector", "ev2", grep[:, e, :], p_A[:, :])
            r_pA.release(0, "ev2", b.val("ev2"))

        if debug:
            wait(b, "scalar", "ev2", b.val("ev2"))
            dma(b, "scalar", "ost", g_dbg[0:8, tok0:tok0 + TG], gem[0:8, :])
        v_grep = b.val("ev2")
        wait(b, "tensor", "ev2", v_grep)
        r_yact.acquire(b, "vector", 0)
        for e in range(8):
            for oc in range(11):
                par = (e * 11 + oc) % 2
                r_wb1.acquire(b, "gpsimd", par)
                vw = dma(b, "gpsimd", "wld", wb1[par][:, :, :], w1d[e, oc])
                wait(b, "tensor", "wld", vw)
                r_pg.acquire(b, "tensor", 0)
                for kc in range(KC):
                    mm(b, "mmg", p_g[:, :], wb1[par][:, kc, :], xln[:, kc, :],
                       start=(kc == 0), stop=(kc == KC - 1), inc=(kc == KC - 1))
                r_wb1.release(par, "mmg", b.val("mmg"))
                par2 = (e * 11 + oc + 1) % 2
                r_wb1.acquire(b, "gpsimd", par2)
                vw2 = dma(b, "gpsimd", "wld", wb1[par2][:, :, :], w1d[e, oc + 11])
                wait(b, "tensor", "wld", vw2)
                r_pu.acquire(b, "tensor", 0)
                for kc in range(KC):
                    mm(b, "mmg", p_u[:, :], wb1[par2][:, kc, :], xln[:, kc, :],
                       start=(kc == 0), stop=(kc == KC - 1), inc=(kc == KC - 1))
                r_wb1.release(par2, "mmg", b.val("mmg"))
                wait(b, "scalar", "mmg")
                r_sg.acquire(b, "scalar", 0)
                act(b, "ev3", sg[:, :], p_g[:, :], AF.Silu)
                r_pg.release(0, "ev3", b.val("ev3"))
                wait(b, "vector", "ev3")
                tt(b, "vector", "ev3", t1[:, :], p_u[:, :], sg[:, :], ALU.mult)
                r_pu.release(0, "ev3", b.val("ev3"))
                r_sg.release(0, "ev3", b.val("ev3"))
                wait(b, "vector", "ev3")
                tt(b, "vector", "ev3", yact[:, e * 11 + oc, :], t1[:, :],
                   grep[:, e, :], ALU.mult)
        v_yact = b.val("ev3")
        if debug:
            wait(b, "scalar", "ev3", v_yact)
            for e in range(8):
                for kc in range(11):
                    dma(b, "scalar", "ost",
                        ya_dbg[kc * 128:(kc + 1) * 128, e, tok0:tok0 + TG],
                        yact[:, e * 11 + kc, :])
            wait(b, "vector", "ost", b.val("ost"))
            wait(b, "tensor", "ost", b.val("ost"))
        wait(b, "tensor", "ev3", v_yact)

        for oc in range(KC):
            par = oc % 2
            pr = oc % 2
            xr = oc % 2
            r_xre.acquire(b, "sync", xr)
            vx = dma(b, "sync", "xld", xre[xr][:, :],
                     x_in[oc * 128:(oc + 1) * 128, tok0:tok0 + TG])
            r_py2.acquire(b, "tensor", pr)
            for e in range(8):
                wpar = (oc * 8 + e) % 2
                r_wb2.acquire(b, "gpsimd", wpar)
                vw = dma(b, "gpsimd", "wld", wb2[wpar][:, :, :], w2d[e, oc])
                wait(b, "tensor", "wld", vw)
                for kc in range(11):
                    mm(b, "mmw", p_y2[pr][:, :], wb2[wpar][:, kc, :],
                       yact[:, e * 11 + kc, :],
                       start=(e == 0 and kc == 0), stop=(e == 7 and kc == 10),
                       inc=(kc == 10))
                r_wb2.release(wpar, "mmw", b.val("mmw"))
            wait(b, "vector", "mmw")
            wait(b, "vector", "xld", vx)
            r_xo.acquire(b, "vector", xr)
            tt(b, "vector", "evR", xo[xr][:, :], p_y2[pr][:, :], xre[xr][:, :], ALU.add)
            v_r = b.val("evR")
            r_py2.release(pr, "evR", v_r)
            r_xre.release(xr, "evR", v_r)
            wait(b, "scalar", "evR")
            dma(b, "scalar", "ost",
                x_out[oc * 128:(oc + 1) * 128, tok0:tok0 + TG], xo[xr][:, :])
            r_xo.release(xr, "ost", b.val("ost"))
            wait(b, "gpsimd", "ost", b.val("ost"))
            wait(b, "sync", "ost", b.val("ost"))
        r_yact.release(0, "mmw", b.val("mmw"))
    wait(b, "scalar", "ost")
    return b.finish()



def build_qkv():
    b = B("qkv")
    b.constf(EPS_LN)
    nc = b.nc
    x_in = nc.dram_tensor("x_in", [D, T], F32, kind="ExternalInput").ap()
    wqk = nc.dram_tensor("wqk", [24, 128, 16, 128], F32, kind="ExternalInput").ap()
    qkv = nc.dram_tensor("qkv", [3072, T], BF16, kind="ExternalOutput").ap()

    ti = make_ln_tiles(b, 512)
    xln = ti["xln"]
    wb = [b.sb(f"wbQ{i}", [128, 16, 128], BF16) for i in range(2)]
    qo = [b.sb(f"qoQ{i}", [128, 512], BF16) for i in range(2)]
    p_y = [b.ps(f"p_yQ{i}", [128, 512]) for i in range(2)]
    r_wb, r_py, r_qo = Rot(2), Rot(2), Rot(2)

    memset(b, "gpsimd", "ini", ti["ones"][:, :], 1.0)
    wait(b, "tensor", "ini")

    for g in range(T // 512):
        tok0 = g * 512
        emit_ln(b, ti, x_in, tok0, 512, dst_guard=("mmq", b.val("mmq")))
        for oc in range(24):
            par = oc % 2
            r_wb.acquire(b, "gpsimd", par)
            vw = dma(b, "gpsimd", "wld", wb[par][:, :, :], wqk[oc])
            wait(b, "tensor", "wld", vw)
            pr = oc % 2
            r_py.acquire(b, "tensor", pr)
            for kc in range(KC):
                mm(b, "mmq", p_y[pr][:, :], wb[par][:, kc, :], xln[:, kc, :],
                   start=(kc == 0), stop=(kc == KC - 1), inc=(kc == KC - 1))
            r_wb.release(par, "mmq", b.val("mmq"))
            wait(b, "scalar", "mmq")
            er = oc % 2
            r_qo.acquire(b, "scalar", er)
            act(b, "evQ", qo[er][:, :], p_y[pr][:, :], AF.Copy)
            r_py.release(pr, "evQ", b.val("evQ"))
            wait(b, "scalar", "evQ")
            dma(b, "scalar", "ost", qkv[oc * 128:(oc + 1) * 128, tok0:tok0 + 512],
                qo[er][:, :])
            r_qo.release(er, "ost", b.val("ost"))
            wait(b, "gpsimd", "ost", b.val("ost"))
    wait(b, "scalar", "ost")
    return b.finish()



def build_s5b():
    b = B("s5b")
    b.constf(EPS_LN)
    nc = b.nc
    x_in = nc.dram_tensor("x_in", [D, T], F32, kind="ExternalInput").ap()
    q_cs = nc.dram_tensor("q_cs", [16, 128, T], BF16, kind="ExternalInput").ap()
    k_cs = nc.dram_tensor("k_cs", [4, 128, T], BF16, kind="ExternalInput").ap()
    v_cs = nc.dram_tensor("v_cs", [4, T, 128], BF16, kind="ExternalInput").ap()
    wod = nc.dram_tensor("wod", [16, 128, 16, 128], F32, kind="ExternalInput").ap()
    x_out = nc.dram_tensor("x_out", [D, T], F32, kind="ExternalOutput").ap()

    HALF = 2048
    big = b.sb("bigS5", [128, KC, HALF], BF16)
    k_sb = b.sb("k_sbS5", [128, 4, HALF], BF16)
    v_sb = b.sb("v_sbS5", [128, 4, 16, 128], BF16)
    qh = b.sb("qhS5", [128, HALF], BF16)
    Et = [b.sb(f"EtS5{i}", [128, 128], BF16) for i in range(2)]
    onesS5 = b.sb("onesS5", [128, 128], BF16)
    bmask = b.sb("bmaskS5", [128, 128], BF16)
    zi = b.sb("ziS5", [128, 128], F32)
    wb = [b.sb(f"wbS5{i}", [128, 16, 128], BF16) for i in range(2)]
    xre = [b.sb(f"xreS5{i}", [128, 512], F32) for i in range(2)]
    xo = [b.sb(f"xoS5{i}", [128, 512], F32) for i in range(2)]
    p_y = [b.ps(f"p_yS5{i}", [128, 512]) for i in range(2)]
    pS = [b.ps(f"pSS5{i}", [128, 128]) for i in range(2)]
    pZ = b.ps("pZS5", [128, 128])
    pO = b.ps("pOS5", [128, 128])

    r_wb, r_py, r_ps, r_et, r_xre, r_xo = (Rot(2) for _ in range(6))
    r_pz, r_po, r_qh, r_kv = Rot(1), Rot(1), Rot(1), Rot(1)

    memset(b, "gpsimd", "ini", onesS5[:, :], 1.0)
    memset(b, "gpsimd", "ini", bmask[:, :], 1.0)
    wait(b, "gpsimd", "ini")
    h, _ = b.inc("ini")
    b.on("gpsimd", lambda q, h=h: q.affine_select(
        out=bmask[:, :], in_=bmask[:, :], pattern=[[1, 128]],
        compare_op=ALU.is_ge, fill=0.0, base=0,
        channel_multiplier=-1).then_inc(h, 1))
    wait(b, "gpsimd", "ini")
    h, _ = b.inc("ini")
    b.on("gpsimd", lambda q, h=h: q.affine_select(
        out=bmask[:, :].rearrange("p (jc jb) -> p jc jb", jc=8),
        in_=bmask[:, :].rearrange("p (jc jb) -> p jc jb", jc=8),
        pattern=[[-16, 8], [0, 16]], compare_op=ALU.is_ge, fill=0.0, base=0,
        channel_multiplier=1).then_inc(h, 1))
    wait(b, "vector", "ini")
    wait(b, "tensor", "ini")

    SCALE = 1.0 / np.sqrt(128.0)

    for half in range(2):
        u0 = half * HALF
        r_kv.acquire(b, "sync", 0)
        for g in range(4):
            dma(b, "sync", "hld", k_sb[:, g, :], k_cs[g, :, u0:u0 + HALF])
            dma(b, "sync", "hld", v_sb[:, g, :, :],
                v_cs[g, u0:u0 + HALF, :].rearrange("(tc p) d -> p tc d", p=128))
        v_kv = b.val("hld")
        wait(b, "tensor", "hld", v_kv)
        for hh in range(16):
            g = hh // 4
            r_qh.acquire(b, "sync", 0)
            v_q = dma(b, "sync", "hld", qh[:, :], q_cs[hh, :, u0:u0 + HALF])
            wait(b, "tensor", "hld", v_q)
            for tc in range(16):
                tsl = slice(tc * 128, (tc + 1) * 128)
                pr = tc % 2
                r_ps.acquire(b, "tensor", pr)
                mm(b, "mms", pS[pr][:, :], k_sb[:, g, tsl], qh[:, tsl],
                   start=True, stop=True, inc=True)
                wait(b, "scalar", "mms")
                er = tc % 2
                r_et.acquire(b, "scalar", er)
                act(b, "evE", Et[er][:, :], pS[pr][:, :], AF.Exp, scale=SCALE)
                r_ps.release(pr, "evE", b.val("evE"))
                wait(b, "vector", "evE")
                tt(b, "vector", "evE", Et[er][:, :], Et[er][:, :], bmask[:, :],
                   ALU.mult)
                wait(b, "tensor", "evE")
                r_pz.acquire(b, "tensor", 0)
                mm(b, None, pZ[:, :], onesS5[:, :], Et[er][:, :], start=True, stop=True)
                r_po.acquire(b, "tensor", 0)
                mm(b, "mma", pO[:, :], v_sb[:, g, tc, :], Et[er][:, :],
                   start=True, stop=True, inc=True)
                r_et.release(er, "mma", b.val("mma"))
                wait(b, "vector", "mma")
                recip(b, "vector", "evA", zi[:, :], pZ[:, :])
                r_pz.release(0, "evA", b.val("evA"))
                wait(b, "vector", "evA")
                tt(b, "vector", "evA", big[:, hh, tsl], pO[:, :], zi[:, :], ALU.mult)
                r_po.release(0, "evA", b.val("evA"))
            r_qh.release(0, "mma", b.val("mma"))
        r_kv.release(0, "mma", b.val("mma"))

        wait(b, "tensor", "evA")
        for oc in range(16):
            par = oc % 2
            r_wb.acquire(b, "gpsimd", par)
            vw = dma(b, "gpsimd", "wld", wb[par][:, :, :], wod[oc])
            wait(b, "tensor", "wld", vw)
            for tt_i in range(4):
                pr = tt_i % 2
                xr = tt_i % 2
                r_xre.acquire(b, "sync", xr)
                vx = dma(b, "sync", "xld", xre[xr][:, :],
                         x_in[oc * 128:(oc + 1) * 128,
                              u0 + tt_i * 512:u0 + (tt_i + 1) * 512])
                r_py.acquire(b, "tensor", pr)
                for kc in range(KC):
                    mm(b, "mmw", p_y[pr][:, :], wb[par][:, kc, :],
                       big[:, kc, tt_i * 512:(tt_i + 1) * 512],
                       start=(kc == 0), stop=(kc == KC - 1), inc=(kc == KC - 1))
                wait(b, "vector", "mmw")
                wait(b, "vector", "xld", vx)
                xw = tt_i % 2
                r_xo.acquire(b, "vector", xw)
                tt(b, "vector", "evR", xo[xw][:, :], p_y[pr][:, :], xre[xr][:, :],
                   ALU.add)
                v_r = b.val("evR")
                r_py.release(pr, "evR", v_r)
                r_xre.release(xr, "evR", v_r)
                wait(b, "scalar", "evR")
                dma(b, "scalar", "ost",
                    x_out[oc * 128:(oc + 1) * 128, u0 + tt_i * 512:u0 + (tt_i + 1) * 512],
                    xo[xw][:, :])
                r_xo.release(xw, "ost", b.val("ost"))
                wait(b, "gpsimd", "ost", b.val("ost"))
                wait(b, "sync", "ost", b.val("ost"))
            r_wb.release(par, "mmw", b.val("mmw"))
        wait(b, "vector", "mmw", b.val("mmw"))
    wait(b, "scalar", "ost")
    return b.finish()



def build_attn_b2(debug=False):
    b = B("attn_b2")
    b.constf(EPS_LN)
    nc = b.nc
    x_in = nc.dram_tensor("x_in", [D, T], F32, kind="ExternalInput").ap()
    wqd = nc.dram_tensor("wqd", [16, 128, 16, 128], F32, kind="ExternalInput").ap()
    wkd = nc.dram_tensor("wkd", [16, 128, 16, 128], F32, kind="ExternalInput").ap()
    wvr = nc.dram_tensor("wvr", [16, 128, 2048], F32, kind="ExternalInput").ap()
    wod = nc.dram_tensor("wod", [16, 128, 16, 128], F32, kind="ExternalInput").ap()
    gmm = nc.dram_tensor("gmm", [128, 1], F32, kind="ExternalInput").ap()
    btm = nc.dram_tensor("btm", [128, 1], F32, kind="ExternalInput").ap()
    x_out = nc.dram_tensor("x_out", [D, T], F32, kind="ExternalOutput").ap()
    qf = nc.dram_tensor("qf", [NB, D, S], BF16, kind="Internal").ap()
    kf = nc.dram_tensor("kf", [NB, D, S], BF16, kind="Internal").ap()
    vt = nc.dram_tensor("vt", [NB, S, D], BF16, kind="Internal").ap()
    if debug:
        q_dbg = nc.dram_tensor("q_dbg", [NB, D, S], BF16, kind="ExternalOutput").ap()
        a_dbg = nc.dram_tensor("a_dbg", [D, T], BF16, kind="ExternalOutput").ap()

    ti = make_ln_tiles(b, 512, with_xln=False)
    big = b.sb("bigB2", [128, KC, S], BF16)
    wb = [b.sb(f"wbB2{i}", [128, 16, 128], BF16) for i in range(2)]
    wvs = b.sb("wvsB2", [128, KC, 512], BF16)
    qh = b.sb("qhB2", [128, S], BF16)
    kh = b.sb("khB2", [128, S], BF16)
    vh = b.sb("vhB2", [128, KC, 128], BF16)
    Et = [b.sb(f"EtB2{i}", [128, 512], BF16) for i in range(2)]
    masks = b.sb("masksB2", [128, 4, 512], BF16)
    zi = b.sb("ziB2", [128, 512], F32)
    qo = [b.sb(f"qoB2{i}", [128, 512], BF16) for i in range(2)]
    gm_s = b.sb("gm_sB2", [128, 1], F32)
    bt_s = b.sb("bt_sB2", [128, 1], F32)
    tmpF = b.sb("tmpFB2", [128, 512], F32)
    tbB = b.sb("tbB2", [128, 512], BF16)
    sq2 = b.sb("sq2B2", [128, 512], BF16)
    muF = b.sb("muFB2", [128, 512], F32)
    msF = b.sb("msFB2", [128, 512], F32)
    vaF = b.sb("vaFB2", [128, 512], F32)
    dF = b.sb("dFB2", [128, 512], F32)
    bvr_z = b.sb("bvrzB2", [128, 2048], F32)
    xre = [b.sb(f"xreB2{i}", [128, 512], F32) for i in range(2)]
    xo = [b.sb(f"xoB2{i}", [128, 512], F32) for i in range(2)]
    p_y = [b.ps(f"p_yB2{i}", [128, 512]) for i in range(2)]
    pS = [b.ps(f"pSB2{i}", [128, 512]) for i in range(2)]
    pZ = b.ps("pZB2", [128, 512])
    pO = b.ps("pOB2", [128, 512])

    r_wb, r_py, r_ps, r_et, r_qo, r_xre, r_xo = (Rot(2) for _ in range(7))
    r_pz, r_po, r_hbuf, r_wvs, r_fm = Rot(1), Rot(1), Rot(1), Rot(1), Rot(1)

    v_b = dma(b, "sync", "xld", gm_s[:, :], gmm)
    v_b = dma(b, "sync", "xld", bt_s[:, :], btm)
    wait(b, "scalar", "xld", v_b)
    memset(b, "gpsimd", "ini", bvr_z[:, :], 0.0)
    memset(b, "gpsimd", "ini", ti["ones"][:, :], 1.0)
    memset(b, "gpsimd", "ini", masks[:, :, :], 1.0)
    wait(b, "gpsimd", "ini")
    for i in range(4):
        h, v = b.inc("ini")
        b.on("gpsimd", lambda q, i=i, h=h: q.affine_select(
            out=masks[:, i, :], in_=masks[:, i, :], pattern=[[1, 512]],
            compare_op=ALU.is_ge, fill=0.0, base=-(i * 128),
            channel_multiplier=-1).then_inc(h, 1))
    wait(b, "vector", "ini")
    wait(b, "tensor", "ini")

    QSC = float(128.0 ** -0.5)

    for b_i in range(NB):
        s0 = b_i * S
        for grp in range(4):
            emit_ln(b, ti, x_in, s0 + grp * 512, 512,
                    dst=big[:, :, grp * 512:(grp + 1) * 512],
                    dst_guard=("mmw", b.val("mmw")))

        for which, wsrc, dst_h in ((0, wqd, qf), (1, wkd, kf)):
            for oc in range(16):
                par = oc % 2
                r_wb.acquire(b, "gpsimd", par)
                vw = dma(b, "gpsimd", "wld", wb[par][:, :, :], wsrc[oc])
                wait(b, "tensor", "wld", vw)
                for tq in range(4):
                    pr = (oc * 4 + tq) % 2
                    r_py.acquire(b, "tensor", pr)
                    for kc in range(KC):
                        mm(b, "mmq", p_y[pr][:, :], wb[par][:, kc, :],
                           big[:, kc, tq * 512:(tq + 1) * 512],
                           start=(kc == 0), stop=(kc == KC - 1), inc=(kc == KC - 1))
                    wait(b, "scalar", "mmq")
                    r_fm.acquire(b, "scalar", 0)
                    act(b, "evQ", tmpF[:, :], p_y[pr][:, :], AF.Identity,
                        bias=bt_s[:, 0:1], scale=gm_s[:, 0:1])
                    r_py.release(pr, "evQ", b.val("evQ"))
                    wait(b, "scalar", "evQ")
                    act(b, "evQ", tbB[:, :], tmpF[:, :], AF.Copy)
                    wait(b, "vector", "evQ")
                    tt(b, "vector", "evQ", sq2[:, :], tbB[:, :], tbB[:, :], ALU.mult)
                    wait(b, "tensor", "evQ")
                    r_ps.acquire(b, "tensor", 0)
                    mm(b, None, pS[0][:, :], ti["ones"][:, :], tbB[:, :],
                       start=True, stop=True)
                    r_ps.acquire(b, "tensor", 1)
                    mm(b, "mmf", pS[1][:, :], ti["ones"][:, :], sq2[:, :],
                       start=True, stop=True, inc=True)
                    wait(b, "scalar", "mmf")
                    act(b, "ev4", muF[:, :], pS[0][:, :], AF.Identity, scale=1.0 / 128)
                    act(b, "ev4", msF[:, :], pS[1][:, :], AF.Identity, scale=1.0 / 128)
                    v_e = b.val("ev4")
                    r_ps.release(0, "ev4", v_e)
                    r_ps.release(1, "ev4", v_e)
                    wait(b, "vector", "ev4")
                    tt(b, "vector", "ev4", vaF[:, :], muF[:, :], muF[:, :], ALU.mult)
                    wait(b, "vector", "ev4")
                    tt(b, "vector", "ev4", vaF[:, :], msF[:, :], vaF[:, :], ALU.subtract)
                    wait(b, "scalar", "ev4")
                    act(b, "ev4", vaF[:, :], vaF[:, :], AF.Sqrt, bias=EPS_LN)
                    wait(b, "vector", "ev4")
                    recip(b, "vector", "ev4", vaF[:, :], vaF[:, :])
                    if which == 0:
                        wait(b, "vector", "ev4")
                        tscalar(b, "vector", "ev4", vaF[:, :], vaF[:, :], QSC, None,
                                ALU.mult)
                    wait(b, "vector", "ev4")
                    tt(b, "vector", "ev4", dF[:, :], tmpF[:, :], muF[:, :], ALU.subtract)
                    wait(b, "vector", "ev4")
                    er = (oc * 4 + tq) % 2
                    r_qo.acquire(b, "vector", er)
                    tt(b, "vector", "evV", qo[er][:, :], dF[:, :], vaF[:, :], ALU.mult)
                    r_fm.release(0, "evV", b.val("evV"))
                    wait(b, "scalar", "evV")
                    dma(b, "scalar", "qst",
                        dst_h[b_i, oc * 128:(oc + 1) * 128, tq * 512:(tq + 1) * 512],
                        qo[er][:, :])
                    r_qo.release(er, "qst", b.val("qst"))
                r_wb.release(par, "mmq", b.val("mmq"))

        for ds in range(4):
            r_wvs.acquire(b, "gpsimd", 0)
            for kc in range(KC):
                dma(b, "gpsimd", "wld", wvs[:, kc, :],
                    wvr[kc, :, ds * 512:(ds + 1) * 512])
            vw = b.val("wld")
            wait(b, "tensor", "wld", vw)
            for tc in range(16):
                pr = (ds * 16 + tc) % 2
                r_py.acquire(b, "tensor", pr)
                for kc in range(KC):
                    mm(b, "mmq", p_y[pr][:, :], big[:, kc, tc * 128:(tc + 1) * 128],
                       wvs[:, kc, :],
                       start=(kc == 0), stop=(kc == KC - 1), inc=(kc == KC - 1))
                wait(b, "scalar", "mmq")
                er = (ds * 16 + tc) % 2
                r_qo.acquire(b, "scalar", er)
                act(b, "evQ", qo[er][:, :], p_y[pr][:, :], AF.Copy)
                r_py.release(pr, "evQ", b.val("evQ"))
                wait(b, "scalar", "evQ")
                dma(b, "scalar", "qst",
                    vt[b_i, tc * 128:(tc + 1) * 128, ds * 512:(ds + 1) * 512],
                    qo[er][:, :])
                r_qo.release(er, "qst", b.val("qst"))
            r_wvs.release(0, "mmq", b.val("mmq"))

        wait(b, "sync", "qst", b.val("qst"))
        for hh in range(16):
            r_hbuf.acquire(b, "sync", 0)
            dma(b, "sync", "hld", qh[:, :], qf[b_i, hh * 128:(hh + 1) * 128, :])
            dma(b, "sync", "hld", kh[:, :], kf[b_i, hh * 128:(hh + 1) * 128, :])
            v_h = dma(b, "sync", "hld",
                      vh[:, :, :],
                      vt[b_i].rearrange("(c p) d -> p c d", p=128)[:, :, hh * 128:(hh + 1) * 128])
            wait(b, "tensor", "hld", v_h)
            wait(b, "scalar", "hld", v_h)
            for qt in range(4):
                nkc = 4 * (qt + 1)
                r_pz.acquire(b, "tensor", 0)
                r_po.acquire(b, "tensor", 0)
                for kc in range(nkc):
                    pr = kc % 2
                    r_ps.acquire(b, "tensor", pr)
                    mm(b, "mms", pS[pr][:, :], kh[:, kc * 128:(kc + 1) * 128],
                       qh[:, qt * 512:(qt + 1) * 512], start=True, stop=True, inc=True)
                    wait(b, "scalar", "mms")
                    er = kc % 2
                    r_et.acquire(b, "scalar", er)
                    act(b, "evE", Et[er][:, :], pS[pr][:, :], AF.Square)
                    r_ps.release(pr, "evE", b.val("evE"))
                    if kc >= 4 * qt:
                        wait(b, "vector", "evE")
                        tt(b, "vector", "evE", Et[er][:, :], Et[er][:, :],
                           masks[:, kc - 4 * qt, :], ALU.mult)
                    wait(b, "tensor", "evE")
                    mm(b, None, pZ[:, :], ti["ones"][:, :], Et[er][:, :],
                       start=(kc == 0), stop=(kc == nkc - 1))
                    mm(b, "mma", pO[:, :], vh[:, kc, :], Et[er][:, :],
                       start=(kc == 0), stop=(kc == nkc - 1), inc=True)
                    r_et.release(er, "mma", b.val("mma"))
                wait(b, "vector", "mma")
                tscalar(b, "vector", "evA", zi[:, :], pZ[:, :], 1e-5, None, ALU.add)
                r_pz.release(0, "evA", b.val("evA"))
                wait(b, "vector", "evA")
                recip(b, "vector", "evA", zi[:, :], zi[:, :])
                wait(b, "vector", "evA")
                tt(b, "vector", "evA", big[:, hh, qt * 512:(qt + 1) * 512],
                   pO[:, :], zi[:, :], ALU.mult)
                v_evA = b.val("evA")
                r_po.release(0, "evA", v_evA)
            r_hbuf.release(0, "mma", b.val("mma"))
        if debug:
            wait(b, "scalar", "evA")
            for c in range(KC):
                dma(b, "scalar", "qst", a_dbg[c * 128:(c + 1) * 128, s0:s0 + S],
                    big[:, c, :])
            wait(b, "vector", "qst", b.val("qst"))

        wait(b, "tensor", "evA")
        for oc in range(16):
            par = oc % 2
            r_wb.acquire(b, "gpsimd", par)
            vw = dma(b, "gpsimd", "wld", wb[par][:, :, :], wod[oc])
            wait(b, "tensor", "wld", vw)
            for tt_i in range(4):
                pr = (oc * 4 + tt_i) % 2
                xr = (oc * 4 + tt_i) % 2
                r_xre.acquire(b, "sync", xr)
                vx = dma(b, "sync", "xld", xre[xr][:, :],
                         x_in[oc * 128:(oc + 1) * 128,
                              s0 + tt_i * 512:s0 + (tt_i + 1) * 512])
                r_py.acquire(b, "tensor", pr)
                for kc in range(KC):
                    mm(b, "mmw", p_y[pr][:, :], wb[par][:, kc, :],
                       big[:, kc, tt_i * 512:(tt_i + 1) * 512],
                       start=(kc == 0), stop=(kc == KC - 1), inc=(kc == KC - 1))
                wait(b, "vector", "mmw")
                wait(b, "vector", "xld", vx)
                xw = (oc * 4 + tt_i) % 2
                r_xo.acquire(b, "vector", xw)
                tt(b, "vector", "evR", xo[xw][:, :], p_y[pr][:, :], xre[xr][:, :],
                   ALU.add)
                v_r = b.val("evR")
                r_py.release(pr, "evR", v_r)
                r_xre.release(xr, "evR", v_r)
                wait(b, "scalar", "evR")
                dma(b, "scalar", "ost",
                    x_out[oc * 128:(oc + 1) * 128, s0 + tt_i * 512:s0 + (tt_i + 1) * 512],
                    xo[xw][:, :])
                r_xo.release(xw, "ost", b.val("ost"))
                wait(b, "gpsimd", "ost", b.val("ost"))
                wait(b, "sync", "ost", b.val("ost"))
            r_wb.release(par, "mmw", b.val("mmw"))
    if debug:
        wait(b, "gpsimd", "qst")
        dma(b, "gpsimd", "ost", q_dbg, qf)
    wait(b, "scalar", "ost")
    wait(b, "gpsimd", "ost")
    return b.finish()



def shard_x(x_full):
    out = []
    for i in range(NCORES):
        xi = x_full[2 * i:2 * i + 2]
        xi = np.transpose(xi, (2, 0, 1)).reshape(D, T)
        out.append(np.ascontiguousarray(xi))
    return out


def unshard_x(parts):
    full = np.empty((16, S, D), np.float32)
    for i in range(NCORES):
        xi = np.asarray(parts[i]).reshape(D, NB, S)
        full[2 * i:2 * i + 2] = np.transpose(xi, (1, 2, 0))
    return full


def wtile(W, nout_c):
    nin = W.shape[1]
    wt = W.T.reshape(nin // 128, 128, nout_c, 128)
    return np.ascontiguousarray(wt.transpose(2, 1, 0, 3))


def bmat(v):
    return np.ascontiguousarray(v.reshape(-1, 128).T)


_CACHE = {}


def _get(name, builder):
    if name not in _CACHE:
        _CACHE[name] = builder()
    return _CACHE[name]


def run_mlp(x_parts, W1, b1, W2, b2, W3, b3):
    nc = _get("mlp", build_mlp)
    shared = dict(w1d=wtile(W1, 32), b1m=bmat(b1), w2d=wtile(W2, 32), b2m=bmat(b2),
                  w3d=wtile(W3, 16), b3m=bmat(b3))
    in_maps = [dict(x_in=x_parts[i], **shared) for i in range(NCORES)]
    res = run_bass_kernel_spmd(nc, in_maps, core_ids=list(range(NCORES)))
    return [np.asarray(res.results[i]["x_out"]) for i in range(NCORES)]


def s5_host(x4_full, qkv_parts):
    Q = np.stack([np.asarray(p).reshape(3072, NB, S) for p in qkv_parts], axis=1)
    Q = Q.reshape(3072, 16, S)
    maps = []
    for i in range(NCORES):
        cs = slice(256 * i, 256 * (i + 1))
        Qu = np.ascontiguousarray(np.transpose(Q[:, :, cs], (0, 2, 1))).reshape(3072, T)
        q_cs = np.empty((16, 128, T), dtype=Qu.dtype)
        k_cs = np.empty((4, 128, T), dtype=Qu.dtype)
        v_cs = np.empty((4, T, 128), dtype=Qu.dtype)
        for g in range(4):
            for j in range(4):
                q_cs[g * 4 + j] = Qu[g * 768 + j * 128:g * 768 + (j + 1) * 128]
            k_cs[g] = Qu[g * 768 + 512:g * 768 + 640]
            v_cs[g] = np.ascontiguousarray(Qu[g * 768 + 640:g * 768 + 768].T)
        xc = x4_full[:, cs, :]
        x_cs = np.ascontiguousarray(np.transpose(xc, (2, 1, 0)).reshape(D, T))
        maps.append(dict(x_in=x_cs, q_cs=np.ascontiguousarray(q_cs),
                         k_cs=np.ascontiguousarray(k_cs), v_cs=v_cs))
    return maps


def cs_unshard(parts):
    full = np.empty((16, S, D), np.float32)
    for i in range(NCORES):
        xi = np.asarray(parts[i]).reshape(D, 256, 16)
        full[:, 256 * i:256 * (i + 1), :] = np.transpose(xi, (2, 1, 0))
    return full


def _run(name, builder, in_maps):
    nc = _get(name, builder)
    return run_bass_kernel_spmd(nc, in_maps, core_ids=list(range(NCORES)))


def kernel(**inputs):
    inp = inputs
    parts = shard_x(np.ascontiguousarray(inp["L0_attn_a_x"], dtype=np.float32))

    sh = dict(
        wqd=wtile(inp["L0_attn_a_Wq"], 16), bqm=bmat(inp["L0_attn_a_bq"]),
        wkd=wtile(inp["L0_attn_a_Wk"], 16), bkm=bmat(inp["L0_attn_a_bk"]),
        wvr=np.ascontiguousarray(inp["L0_attn_a_Wv"].T.reshape(16, 128, 2048)),
        bvb=np.ascontiguousarray(np.broadcast_to(inp["L0_attn_a_bv"], (128, 2048))),
        wod=wtile(inp["L0_attn_a_Wo"], 16), bom=bmat(inp["L0_attn_a_bo"]),
    )
    res = _run("attn_a", build_attn_a, [dict(x_in=parts[i], **sh) for i in range(NCORES)])
    parts = [np.asarray(res.results[i]["x_out"]) for i in range(NCORES)]

    moe_sh = dict(
        rtr=np.ascontiguousarray(inp["L0_ffn_router_w"].T.reshape(16, 128, 8)),
        w1d=np.stack([wtile(inp["L0_ffn_w1"][e], 22) for e in range(8)]),
        w2d=np.stack([wtile(inp["L0_ffn_w2"][e], 16) for e in range(8)]),
    )
    res = _run("moe", build_moe_dense, [dict(x_in=parts[i], **moe_sh) for i in range(NCORES)])
    parts = [np.asarray(res.results[i]["x_out"]) for i in range(NCORES)]

    sh = dict(
        wqd=wtile(inp["L0_attn_b_Wq"], 16), bqm=bmat(inp["L0_attn_b_bq"]),
        wkr=np.ascontiguousarray(inp["L0_attn_b_Wk"].T.reshape(16, 128, 512)),
        wvr2=np.ascontiguousarray(inp["L0_attn_b_Wv"].T.reshape(16, 128, 512)),
        bkb=np.ascontiguousarray(np.broadcast_to(inp["L0_attn_b_bk"], (128, 512))),
        bvb=np.ascontiguousarray(np.broadcast_to(inp["L0_attn_b_bv"], (128, 512))),
        wod=wtile(inp["L0_attn_b_Wo"], 16),
    )
    res = _run("attn_b", build_attn_b, [dict(x_in=parts[i], **sh) for i in range(NCORES)])
    parts = [np.asarray(res.results[i]["x_out"]) for i in range(NCORES)]

    res = _run("moe", build_moe_dense, [dict(x_in=parts[i], **moe_sh) for i in range(NCORES)])
    parts = [np.asarray(res.results[i]["x_out"]) for i in range(NCORES)]

    wqk = wtile(inp["L1_attn_a_weight_qkv"], 24)
    res = _run("qkv", build_qkv, [dict(x_in=parts[i], wqk=wqk) for i in range(NCORES)])
    qkv_parts = [np.asarray(res.results[i]["qkv"]) for i in range(NCORES)]

    x4_full = unshard_x(parts)
    maps = s5_host(x4_full, qkv_parts)
    wod5 = wtile(inp["L1_attn_a_weight_proj"], 16)
    for m in maps:
        m["wod"] = wod5
    res = _run("s5b", build_s5b, maps)
    cs_parts = [np.asarray(res.results[i]["x_out"]) for i in range(NCORES)]

    mlp_sh = dict(w1d=wtile(inp["L1_ffn_W1"], 32), b1m=bmat(inp["L1_ffn_b1"]),
                  w2d=wtile(inp["L1_ffn_W2"], 32), b2m=bmat(inp["L1_ffn_b2"]),
                  w3d=wtile(inp["L1_ffn_W3"], 16), b3m=bmat(inp["L1_ffn_b3"]))
    res = _run("mlp", build_mlp, [dict(x_in=cs_parts[i], **mlp_sh) for i in range(NCORES)])
    cs_parts = [np.asarray(res.results[i]["x_out"]) for i in range(NCORES)]

    parts = shard_x(cs_unshard(cs_parts))

    sh = dict(
        wqd=wtile(inp["L1_attn_b_Wq"], 16),
        wkd=wtile(inp["L1_attn_b_Wk"], 16),
        wvr=np.ascontiguousarray(inp["L1_attn_b_Wv"].T.reshape(16, 128, 2048)),
        wod=wtile(inp["L1_attn_b_Wo"], 16),
        gmm=np.ascontiguousarray(inp["L1_attn_b_gamma"].astype(np.float32)[:, None]),
        btm=np.ascontiguousarray(inp["L1_attn_b_beta"].astype(np.float32)[:, None]),
    )
    res = _run("attn_b2", build_attn_b2, [dict(x_in=parts[i], **sh) for i in range(NCORES)])
    parts = [np.asarray(res.results[i]["x_out"]) for i in range(NCORES)]

    res = _run("mlp", build_mlp, [dict(x_in=parts[i], **mlp_sh) for i in range(NCORES)])
    parts = [np.asarray(res.results[i]["x_out"]) for i in range(NCORES)]

    return unshard_x(parts)
